# version 08
# baseline (speedup 1.0000x reference)
import numpy as np
import ml_dtypes
from contextlib import ExitStack

import concourse.bass as bass
import concourse.mybir as mybir
from concourse.bass_utils import run_bass_kernel_spmd

F32 = mybir.dt.float32
BF16 = mybir.dt.bfloat16
AF = mybir.ActivationFunctionType
ALU = mybir.AluOpType
AX = mybir.AxisListType

D = 1024
LK = 8192
NQ = 2048
CC = 8
KB = 16
QKB = 4
PAIRS = 8
HEADS = 16
NKT = 64
CSH = 10.0
EPS = 1e-5
FC = 4
NE = 9
VS = 66
QBN = 4
QL = 512
NGQ = PAIRS * NKT
NPQ = QBN * PAIRS

LNA = 5 * KB
TPA = 8 * 4 * KB
XNA = 8 * 4 * KB
RIA = KB


STAGE = "full"


def build_program():
    nc = bass.Bass("TRN2", target_bir_lowering=False,
                   detect_race_conditions=False)

    xb = nc.dram_tensor("xb", [LK, D], F32, kind="ExternalInput")
    WqT = nc.dram_tensor("WqT", [D, D], F32, kind="ExternalInput")
    WkT = nc.dram_tensor("WkT", [D, D], F32, kind="ExternalInput")
    WvT = nc.dram_tensor("WvT", [D, D], F32, kind="ExternalInput")
    WoT = nc.dram_tensor("WoT", [D, D], F32, kind="ExternalInput")
    bo = nc.dram_tensor("bo", [D], F32, kind="ExternalInput")
    Ws1 = nc.dram_tensor("Ws1", [D, 512], F32, kind="ExternalInput")
    Ws2 = nc.dram_tensor("Ws2", [512, D], F32, kind="ExternalInput")
    W1 = nc.dram_tensor("W1", [8, D, 512], F32, kind="ExternalInput")
    W2 = nc.dram_tensor("W2", [8, 512, D], F32, kind="ExternalInput")
    Wr = nc.dram_tensor("Wr", [D, 8], F32, kind="ExternalInput")
    out_d = nc.dram_tensor("out", [NQ, D], F32, kind="ExternalOutput")

    KT_dram = nc.dram_tensor("KT_scr", [128, PAIRS, LK], BF16)
    V_dram = nc.dram_tensor("V_scr", [128, HEADS, NKT * VS], BF16)
    QT_dram = nc.dram_tensor("QT_scr", [128, PAIRS, NQ], BF16)
    rv_dram = nc.dram_tensor("rv_scr", [2, D], F32)
    Wf1_dram = nc.dram_tensor("Wf1_scr", [NE, 128, CC * 512], BF16)
    Wf2_dram = nc.dram_tensor("Wf2_scr", [NE, 128, FC * D], BF16)

    ident_dram = nc.inline_tensor(np.eye(128, dtype=ml_dtypes.bfloat16), "ident_c")

    g = ExitStack()
    with g:
        sem = lambda n: g.enter_context(nc.semaphore(n))

        ident = g.enter_context(nc.sbuf_tensor("ident", [128, 128], BF16))
        biasC = g.enter_context(nc.sbuf_tensor("biasC", [128, 1], F32))
        epsc = g.enter_context(nc.sbuf_tensor("epsc", [128, 1], F32))
        junk0 = g.enter_context(nc.sbuf_tensor("junk0", [128, 8], F32))
        scr2 = g.enter_context(nc.sbuf_tensor("scr2", [128, D], F32))

        s_ph = sem("s_ph")
        s_x = sem("s_x")
        s_wc = sem("s_wc")
        s_ln = sem("s_ln")
        s_tp = sem("s_tp")
        s_xnb = sem("s_xnb")
        s_kmm = sem("s_kmm")
        s_kev = sem("s_kev")
        s_stg = sem("s_stg")
        s_rinv = sem("s_rinv")
        s_qld = sem("s_qld")
        s_kvld = sem("s_kvld")
        s_sc = sem("s_sc")
        s_exp = sem("s_exp")
        s_av = sem("s_av")
        s_nrm = sem("s_nrm")
        s_riv = sem("s_riv")
        s_rbc = sem("s_rbc")
        s_nst = sem("s_nst")
        s_ots = sem("s_ots")
        s_cmm = sem("s_cmm")
        s_cbo = sem("s_cbo")
        s_cev = sem("s_cev")
        s_x1l = sem("s_x1l")
        s_d2s = sem("s_d2s")
        s_dr = sem("s_dr")
        s_ln2 = sem("s_ln2")
        s_tp2 = sem("s_tp2")
        s_xn2 = sem("s_xn2")
        s_lg = sem("s_lg")
        s_lge = sem("s_lge")
        s_rt = sem("s_rt")
        s_wld = sem("s_wld")
        s_w2l = sem("s_w2l")
        s_hmm = sem("s_hmm")
        s_hrl = sem("s_hrl")
        s_hsq = sem("s_hsq")
        s_omm = sem("s_omm")
        s_oev = sem("s_oev")
        s_od = sem("s_od")
        s_qst = sem("s_qst")
        s_bo = sem("s_bo")

        block = g.enter_context(nc.Block())

        a = ExitStack()
        asb = lambda n, s, d: a.enter_context(nc.sbuf_tensor(n, s, d))
        aps = lambda n, s, d: a.enter_context(nc.psum_tensor(n, s, d))
        WqT_sb = asb("WqT_sb", [128, CC * D], BF16)
        WkT_sb = asb("WkT_sb", [128, CC * D], BF16)
        WvT_sb = asb("WvT_sb", [128, CC * D], BF16)
        xblk = [asb(f"xblk{i}", [128, 4 * D], F32) for i in range(2)]
        xn_t = [asb(f"xn_t{i}", [128, D], BF16) for i in range(4)]
        scrq = asb("scrq", [128, 4 * D], F32)
        st_s1 = asb("st_s1", [128, 4], F32)
        st_s2 = asb("st_s2", [128, 4], F32)
        st_m = asb("st_m", [128, 4], F32)
        st_q = asb("st_q", [128, 4], F32)
        st_lt = asb("st_lt", [128, 4], F32)
        st_t = asb("st_t", [128, 4], F32)
        st_v = asb("st_v", [128, 4], F32)
        st_r = asb("st_r", [128, 4], F32)
        xnT = [asb(f"xnT{i}", [128, CC * 512], BF16) for i in range(2)]
        KT_stage = [asb(f"KT_st{i}", [128, PAIRS * 512], BF16) for i in range(2)]
        V_stage = [asb(f"V_st{i}", [128, HEADS * 4 * VS], BF16) for i in range(2)]
        QT_stage = [asb(f"QT_st{i}", [128, PAIRS * 512], BF16) for i in range(2)]
        ps_T = [aps(f"ps_T{i}", [128, 128], BF16) for i in range(4)]
        ps_M = [aps(f"ps_M{i}", [128, 512], F32) for i in range(4)]

        kvq_sched = []
        for kb in range(KB):
            for pr in range(PAIRS):
                kvq_sched.append((kb, "K", pr))
            for kt in range(4):
                for half in range(2):
                    kvq_sched.append((kb, "V", (kt, half)))
            if kb < QKB:
                for pr in range(PAIRS):
                    kvq_sched.append((kb, "Q", pr))
        kb_end_count = {}
        for i, (kb, kind, argv) in enumerate(kvq_sched):
            kb_end_count[kb] = i + 1
        kb_start = {kb: (kb_end_count[kb - 1] if kb else 0) for kb in range(KB)}

        @block.sync
        def _(sync):
            sync.dma_start(out=ident[:, :], in_=ident_dram.ap()).then_inc(s_x, 16)
            for kb in range(KB):
                if kb >= 2:
                    sync.wait_ge(s_ln, 5 * (kb - 1))
                sync.dma_start(
                    out=xblk[kb % 2][:, :].rearrange("p (s d) -> p s d", s=4),
                    in_=xb.ap().rearrange("(kb s p) d -> kb p s d", p=128, s=4)[kb],
                ).then_inc(s_x, 16)
            sync.sem_inc(s_ph, 1)

        @block.gpsimd
        def _(gp):
            for W, Wsb in ((WqT, WqT_sb), (WkT, WkT_sb), (WvT, WvT_sb)):
                gp.dma_start(
                    out=Wsb[:, :].rearrange("p (cc d) -> p cc d", cc=CC),
                    in_=W.ap().rearrange("(cc p) d -> p cc d", p=128),
                ).then_inc(s_wc, 16)
            for i in range(2):
                vst = V_stage[i][:, :].rearrange("p (h kt c) -> p h kt c", h=HEADS, kt=4)
                gp.memset(vst[:, :, :, 64:65], 1.0)
                gp.memset(vst[:, :, :, 65:66], 0.0)
            for kb in range(KB):
                gp.wait_ge(s_kev, kb_end_count[kb])
                gp.dma_start(
                    out=KT_dram.ap()[:, :, kb * 512:(kb + 1) * 512],
                    in_=KT_stage[kb % 2][:, :].rearrange("p (pr c) -> p pr c", pr=PAIRS),
                ).then_inc(s_stg, 16)
                gp.dma_start(
                    out=V_dram.ap()[:, :, kb * 4 * VS:(kb + 1) * 4 * VS],
                    in_=V_stage[kb % 2][:, :].rearrange("p (h c) -> p h c", h=HEADS),
                ).then_inc(s_stg, 16)
                if kb < QKB:
                    gp.dma_start(
                        out=QT_dram.ap()[:, :, kb * 512:(kb + 1) * 512],
                        in_=QT_stage[kb % 2][:, :].rearrange("p (pr c) -> p pr c", pr=PAIRS),
                    ).then_inc(s_qst, 16)
            gp.wait_ge(s_stg, 32 * KB)
            gp.wait_ge(s_qst, 16 * QKB)
            gp.sem_inc(s_ph, 1)

        @block.vector
        def _(v):
            v.memset(biasC[:, :], -CSH)
            v.memset(epsc[:, :], EPS)
            v.memset(scrq[:, :], 1.0)
            v.memset(junk0[:, :], 1.0)
            for _w in range(3):
                v.reduce_sum(junk0[:, 0:1], scrq[:, 0:D], axis=AX.X)
                v.tensor_tensor(scrq[:, 0:D], scrq[:, 0:D], scrq[:, 0:D], ALU.mult)
                v.tensor_tensor(junk0[:, 4:8], junk0[:, 0:4], junk0[:, 0:4],
                                ALU.subtract)
                v.tensor_scalar(junk0[:, 0:4], junk0[:, 4:8], 0.5, None, ALU.mult)
                v.tensor_copy(junk0[:, 4:8], junk0[:, 0:4])
                v.reciprocal(junk0[:, 0:4], scrq[:, 0:4])
            for kb in range(KB):
                v.wait_ge(s_x, 16 * (kb + 2))
                for t in range(4):
                    xt = xblk[kb % 2][:, t * D:(t + 1) * D]
                    sq = scrq[:, t * D:(t + 1) * D]
                    v.reduce_sum(st_s1[:, t:t + 1], xt, axis=AX.X)
                    v.tensor_tensor(sq, xt, xt, ALU.mult)
                    v.reduce_sum(st_s2[:, t:t + 1], sq, axis=AX.X)
                v.tensor_scalar(st_m[:, :], st_s1[:, :], 1.0 / D, None, ALU.mult)
                v.tensor_scalar(st_q[:, :], st_s2[:, :], 1.0 / D, None, ALU.mult)
                v.tensor_copy(junk0[:, 0:4], st_m[:, :])
                v.tensor_copy(junk0[:, 4:8], st_m[:, :])
                v.tensor_tensor(st_t[:, :], st_m[:, :], st_m[:, :], ALU.mult)
                v.tensor_copy(junk0[:, 0:4], st_t[:, :])
                v.tensor_copy(junk0[:, 4:8], st_t[:, :])
                v.tensor_tensor(st_v[:, :], st_q[:, :], st_t[:, :],
                                ALU.subtract).then_inc(s_ln, 1)
                v.wait_ge(s_rinv, kb + 1)
                for t in range(4):
                    T = 4 * kb + t
                    if T >= 4:
                        v.wait_ge(s_tp, 8 * (T - 3))
                    xt = xblk[kb % 2][:, t * D:(t + 1) * D]
                    v.tensor_scalar(xn_t[t][:, :], xt, st_m[:, t:t + 1],
                                    st_r[:, t:t + 1], ALU.subtract,
                                    ALU.mult).then_inc(s_ln, 1)
                for t in range(4):
                    T = 4 * kb + t
                    if kb >= 2 and t == 0:
                        v.wait_ge(s_kmm, kb_end_count[kb - 2])
                    dst = xnT[kb % 2][:, :].rearrange("p (cc c) -> p cc c", cc=CC)
                    for c in range(CC):
                        gt = T * 8 + c
                        v.wait_ge(s_tp, gt + 1)
                        v.tensor_copy(dst[:, c, t * 128:(t + 1) * 128],
                                      ps_T[gt % 4][:, :]).then_inc(s_xnb, 1)
            v.sem_inc(s_ph, 1)

        @block.scalar
        def _(sc):
            gidx = 0
            for kb in range(KB):
                sc.wait_ge(s_ln, 5 * kb + 1)
                sc.activation(st_lt[:, :], st_v[:, :], AF.Ln, bias=epsc[:, :], scale=1.0)
                sc.activation(st_r[:, :], st_lt[:, :], AF.Exp, bias=0.0,
                              scale=-0.5).then_inc(s_rinv, 1)
                q_seen = False
                for _ in range(kb_end_count[kb] - kb_start[kb]):
                    kbx, kind, argv = kvq_sched[gidx]
                    sc.wait_ge(s_kmm, gidx + 1)
                    if kb >= 2 and gidx == kb_start[kb]:
                        sc.wait_ge(s_stg, 32 * (kb - 1))
                    src = ps_M[gidx % 4]
                    if kind == "K":
                        sc.copy(KT_stage[kb % 2][:, argv * 512:(argv + 1) * 512],
                                src[:, :]).then_inc(s_kev, 1)
                    elif kind == "V":
                        kt, half = argv
                        dst = V_stage[kb % 2][:, :].rearrange(
                            "p (h c) -> p h c", h=HEADS)
                        sc.copy(
                            dst[:, half * 8:(half + 1) * 8, kt * VS:kt * VS + 64],
                            src[:, :].rearrange("p (h c) -> p h c", h=8),
                        ).then_inc(s_kev, 1)
                    else:
                        if not q_seen and kb >= 2:
                            sc.wait_ge(s_qst, 16 * (kb - 1))
                        q_seen = True
                        sc.copy(QT_stage[kb % 2][:, argv * 512:(argv + 1) * 512],
                                src[:, :]).then_inc(s_kev, 1)
                    gidx += 1
            sc.sem_inc(s_ph, 1)

        @block.tensor
        def _(te):
            te.wait_ge(s_x, 16)
            te.wait_ge(s_wc, 48)
            gidx = 0
            for kb in range(KB):
                for t in range(4):
                    T = 4 * kb + t
                    te.wait_ge(s_ln, 5 * kb + t + 2)
                    for c in range(CC):
                        gt = T * 8 + c
                        if gt >= 4:
                            te.wait_ge(s_xnb, gt - 3)
                        te.transpose(ps_T[gt % 4][:, :],
                                     xn_t[t][:, c * 128:(c + 1) * 128],
                                     ident[:, :]).then_inc(s_tp, 1)
                te.wait_ge(s_xnb, 32 * (kb + 1))
                xn = xnT[kb % 2][:, :].rearrange("p (cc c) -> p cc c", cc=CC)
                for _ in range(kb_end_count[kb] - kb_start[kb]):
                    kbx, kind, argv = kvq_sched[gidx]
                    if gidx >= 4:
                        te.wait_ge(s_kev, gidx - 3)
                    dstp = ps_M[gidx % 4]
                    for c in range(CC):
                        if kind == "K":
                            ins = te.matmul(dstp[:, :],
                                            WkT_sb[:, c * D + argv * 128: c * D + (argv + 1) * 128],
                                            xn[:, c, :], start=(c == 0), stop=(c == CC - 1))
                        elif kind == "V":
                            kt, half = argv
                            ins = te.matmul(dstp[:, :],
                                            xn[:, c, kt * 128:(kt + 1) * 128],
                                            WvT_sb[:, c * D + half * 512: c * D + (half + 1) * 512],
                                            start=(c == 0), stop=(c == CC - 1))
                        else:
                            ins = te.matmul(dstp[:, :],
                                            WqT_sb[:, c * D + argv * 128: c * D + (argv + 1) * 128],
                                            xn[:, c, :], start=(c == 0), stop=(c == CC - 1))
                        if c == CC - 1:
                            ins.then_inc(s_kmm, 1)
                    gidx += 1
            te.wait_ge(s_kev, len(kvq_sched))
            te.sem_inc(s_ph, 1)

        PH = 5

        a.close()
        L = ExitStack()
        lsb = lambda n, sh, d: L.enter_context(nc.sbuf_tensor(n, sh, d))
        lps = lambda n, sh, d: L.enter_context(nc.psum_tensor(n, sh, d))

        QT_b = [lsb(f"QT_b{i}", [128, PAIRS * 512], BF16) for i in range(2)]
        KT_buf = [lsb(f"KT_b{i}", [128, LK], BF16) for i in range(2)]
        V_buf = [lsb(f"V_b{i}", [128, 2 * NKT * VS], BF16) for i in range(2)]
        P_buf = lsb("P_b", [128, 4 * 1024], BF16)
        OT_stage = [lsb(f"OT_st{i}", [64, 1024], BF16) for i in range(2)]
        OTF32 = lsb("OTF32", [VS, 1024], F32)
        rinv = lsb("rinv", [65, 1024], F32)
        rb_sb = lsb("rb_sb", [64, 1024], F32)
        OT_pack = [lsb(f"OT_pk{i}", [128, PAIRS * 512], BF16) for i in range(2)]
        WoT_sb = lsb("WoT_sb", [128, PAIRS * D], BF16)
        bobc = lsb("bobc", [128, D], F32)
        x1 = lsb("x1", [128, 4 * D], F32)
        xn2_t = [lsb(f"xn2_t{i}", [128, D], BF16) for i in range(2)]
        xn2T = lsb("xn2T", [128, CC * 512], BF16)
        d2_s1 = lsb("d2_s1", [128, 4], F32)
        d2_s2 = lsb("d2_s2", [128, 4], F32)
        d2_m = lsb("d2_m", [128, 4], F32)
        d2_q = lsb("d2_q", [128, 4], F32)
        d2_lt = lsb("d2_lt", [128, 4], F32)
        d2_t = lsb("d2_t", [128, 4], F32)
        d2_v = lsb("d2_v", [128, 4], F32)
        d2_r = lsb("d2_r", [128, 4], F32)
        Wr_sb = lsb("Wr_sb", [128, CC * 8], BF16)
        logits = lsb("logits", [128, 32], F32)
        ismax = lsb("ismax", [128, 32], F32)
        maskd = lsb("maskd", [128, 32], F32)
        nmax = lsb("nmax", [128, 4], F32)
        m1c = lsb("m1c", [128, 4], F32)
        v2c = lsb("v2c", [128, 4], F32)
        dcol = lsb("dcol", [128, 4], F32)
        ecol = lsb("ecol", [128, 4], F32)
        e2col = lsb("e2col", [128, 4], F32)
        e3col = lsb("e3col", [128, 4], F32)
        gcol = lsb("gcol", [128, 4], F32)
        g2col = lsb("g2col", [128, 4], F32)
        g3col = lsb("g3col", [128, 4], F32)
        prob_all = lsb("prob_all", [128, 32], F32)
        W1_sb = [lsb(f"W1_sb{i}", [128, CC * 512], BF16) for i in range(2)]
        W2_sb = lsb("W2_sb", [128, FC * D], BF16)
        H_sb = lsb("H_sb", [128, FC * 512], BF16)
        r_bf = lsb("r_bf", [128, 512], BF16)
        tmpO = lsb("tmpO", [128, 512], F32)

        ps_S = [lps(f"ps_S{i}", [128, 1024], F32) for i in range(2)]
        ps_OA = lps("ps_OA", [128, 512], F32)
        ps_OB = lps("ps_OB", [128, 512], F32)
        ps_P6 = lps("ps_P6", [128, 512], F32)
        ps_P7 = lps("ps_P7", [128, 512], F32)

        out_r = out_d.ap().rearrange("(tt p) d -> p tt d", p=128)
        x1r = x1[:, :].rearrange("p (qt d) -> p qt d", qt=4)
        xn2 = xn2T[:, :].rearrange("p (cc c) -> p cc c", cc=CC)

        NTE = 128
        NHB = NE * FC
        NOB = NE * 8

        def te_slot(i):
            return 4 * i + 3.5

        def pe_items(te, qb):
            items = []
            for _ in range(4):
                items.append(lambda: None)
            for blk in range(8):
                def f(blk=blk):
                    qt, dmb = divmod(blk, 2)
                    if blk == 0:
                        if STAGE == "full":
                            te.wait_ge(s_oev, NOB * qb)
                        te.wait_ge(s_ots, 32 * 8 * (qb + 1))
                        te.wait_ge(s_wc, 80)
                    if blk >= 2:
                        te.wait_ge(s_cev, 8 * qb + blk - 1)
                    dstp = ps_P6 if blk % 2 == 0 else ps_P7
                    for pr in range(PAIRS):
                        ins = te.matmul(dstp[:, :],
                                        OT_pack[qb % 2][:, pr * 512 + qt * 128: pr * 512 + (qt + 1) * 128],
                                        WoT_sb[:, pr * D + dmb * 512: pr * D + (dmb + 1) * 512],
                                        start=(pr == 0), stop=(pr == PAIRS - 1))
                    ins.then_inc(s_cmm, 1)
                items.append(f)
            for b in range(4):
                items.append(lambda: None)
            for qt in range(4):
                def f(qt=qt):
                    if qt == 0:
                        te.wait_ge(s_cev, 8 * (qb + 1))
                        te.wait_ge(s_wc, 80)
                    if qt >= 2:
                        te.wait_ge(s_lge, 4 * qb + qt - 1)
                    te.wait_ge(s_xn2, 64 * qb + 16 * (qt + 1))
                    dst = ps_P7[:, (qt % 2) * 16:(qt % 2) * 16 + 8]
                    for cc in range(CC):
                        ins = te.matmul(dst,
                                        xn2[:, cc, qt * 128:(qt + 1) * 128],
                                        Wr_sb[:, cc * 8:(cc + 1) * 8],
                                        start=(cc == 0), stop=(cc == CC - 1))
                    ins.then_inc(s_lg, 1)
                items.append(f)
            for e in range(NE):
                we = qb * NE + e
                for fc in range(FC):
                    def f(e=e, we=we, fc=fc):
                        hb = we * FC + fc
                        if fc == 0:
                            te.wait_ge(s_wld, 16 * (we + 1))
                            te.wait_ge(s_xn2, 64 * (qb + 1))
                            if e == 0:
                                te.wait_ge(s_cev, 8 * qb + 7)
                            else:
                                te.wait_ge(s_hsq, FC * we)
                        if hb >= 1:
                            te.wait_ge(s_hrl, hb)
                        for cc in range(CC):
                            ins = te.matmul(ps_P6[:, :],
                                            W1_sb[we % 2][:, cc * 512 + fc * 128:
                                                          cc * 512 + (fc + 1) * 128],
                                            xn2[:, cc, :],
                                            start=(cc == 0), stop=(cc == CC - 1))
                        ins.then_inc(s_hmm, 1)
                    items.append(f)
                for blk in range(8):
                    def f(e=e, we=we, blk=blk):
                        ob = we * 8 + blk
                        tt, dmb = divmod(blk, 2)
                        if blk == 0:
                            te.wait_ge(s_hsq, FC * (we + 1))
                            te.wait_ge(s_w2l, 16 * (we + 1))
                            if e == 0:
                                te.wait_ge(s_lge, 4 * (qb + 1))
                        if ob >= 1:
                            te.wait_ge(s_oev, ob)
                        for fc in range(FC):
                            ins = te.matmul(ps_P7[:, :],
                                            H_sb[:, fc * 512 + tt * 128:
                                                 fc * 512 + (tt + 1) * 128],
                                            W2_sb[:, fc * D + dmb * 512:
                                                  fc * D + (dmb + 1) * 512],
                                            start=(fc == 0), stop=(fc == FC - 1))
                        ins.then_inc(s_omm, 1)
                    items.append(f)
            assert len(items) == NTE, len(items)
            if STAGE in ("BC", "BCSN"):
                items = [it if 4 <= i < 12 else (lambda: None)
                         for i, it in enumerate(items)]
            elif STAGE == "BCD":
                items = [it if 4 <= i < 20 else (lambda: None)
                         for i, it in enumerate(items)]
            elif STAGE in ("BCD1", "BCT"):
                items = [it if 4 <= i < 16 else (lambda: None)
                         for i, it in enumerate(items)]
            return items

        def v_items(v, qb):
            items = []
            want_d = STAGE in ("BCD", "BCD1", "BCT", "BCSN", "full")
            want_tc = STAGE != "BCSN"
            want_d2 = STAGE in ("BCD", "full")
            want_e = STAGE == "full"
            for qt in range(4):
                def f(qt=qt):
                    if qt == 0:
                        v.wait_ge(s_x1l, 16 * (qb + 1))
                        v.wait_ge(s_bo, 16)
                    v.tensor_tensor(x1r[:, qt, :], x1r[:, qt, :], bobc[:, :],
                                    ALU.add).then_inc(s_cbo, 1)
                items.append((te_slot(4) - 1.0 + qt * 0.01, f))
            for blk in range(8):
                def f(blk=blk):
                    qt, dmb = divmod(blk, 2)
                    v.wait_ge(s_cmm, 8 * qb + blk + 1)
                    srcp = ps_P6 if blk % 2 == 0 else ps_P7
                    sl = x1[:, qt * D + dmb * 512: qt * D + (dmb + 1) * 512]
                    v.tensor_tensor(sl, srcp[:, :], sl, ALU.add).then_inc(s_cev, 1)
                items.append((te_slot(4 + blk) + 0.2, f))
            for t in range(4 if want_d else 0):
                def f(t=t):
                    xt = x1r[:, t, :]
                    sq = scr2[:, :]
                    v.reduce_sum(d2_s1[:, t:t + 1], xt, axis=AX.X)
                    v.tensor_tensor(sq, xt, xt, ALU.mult)
                    v.reduce_sum(d2_s2[:, t:t + 1], sq, axis=AX.X)
                items.append((te_slot(5 + 2 * t) + 0.4, f))

            def f_stats():
                v.tensor_scalar(d2_m[:, :], d2_s1[:, :], 1.0 / D, None, ALU.mult)
                v.tensor_scalar(d2_q[:, :], d2_s2[:, :], 1.0 / D, None, ALU.mult)
                v.tensor_copy(junk0[:, 0:4], d2_m[:, :])
                v.tensor_copy(junk0[:, 4:8], d2_m[:, :])
                v.tensor_tensor(d2_t[:, :], d2_m[:, :], d2_m[:, :], ALU.mult)
                v.tensor_copy(junk0[:, 0:4], d2_t[:, :])
                v.tensor_copy(junk0[:, 4:8], d2_t[:, :])
                v.tensor_tensor(d2_v[:, :], d2_q[:, :], d2_t[:, :],
                                ALU.subtract).then_inc(s_d2s, 1)
            if want_d:
                items.append((te_slot(11) + 0.5, f_stats))
            for ti in range(4 if want_d else 0):
                def f(ti=ti):
                    if ti == 0 and STAGE not in ("BCT", "BCSN"):
                        v.wait_ge(s_dr, qb + 1)
                    if ti >= 2:
                        v.wait_ge(s_xn2, 64 * qb + 16 * (ti - 1))
                    v.tensor_scalar(xn2_t[ti % 2][:, :], x1r[:, ti, :],
                                    d2_m[:, ti:ti + 1], d2_r[:, ti:ti + 1],
                                    ALU.subtract, ALU.mult).then_inc(s_ln2, 1)
                items.append((te_slot(12 + ti) - 0.5, f))

            for qt in range(4 if want_d2 else 0):
                def f(qt=qt):
                    v.wait_ge(s_lg, 4 * qb + qt + 1)
                    v.tensor_copy(logits[:, qt * 8:(qt + 1) * 8],
                                  ps_P7[:, (qt % 2) * 16:(qt % 2) * 16 + 8],
                                  ).then_inc(s_lge, 1)
                items.append((te_slot(16 + qt) + 0.2, f))

            def f_rt1():
                for qt in range(4):
                    v.tensor_reduce(m1c[:, qt:qt + 1],
                                    logits[:, qt * 8:(qt + 1) * 8], AX.X, ALU.max)
                v.tensor_copy(junk0[:, 0:4], m1c[:, :])
                v.tensor_copy(junk0[:, 4:8], m1c[:, :])
                for qt in range(4):
                    v.tensor_scalar(ismax[:, qt * 8:(qt + 1) * 8],
                                    logits[:, qt * 8:(qt + 1) * 8],
                                    m1c[:, qt:qt + 1], None, ALU.is_equal)
                for qt in range(4):
                    v.tensor_reduce(nmax[:, qt:qt + 1],
                                    ismax[:, qt * 8:(qt + 1) * 8], AX.X, ALU.add)
                for qt in range(4):
                    v.tensor_scalar(maskd[:, qt * 8:(qt + 1) * 8],
                                    ismax[:, qt * 8:(qt + 1) * 8], -1e30, None,
                                    ALU.mult)
                for qt in range(4):
                    msl = maskd[:, qt * 8:(qt + 1) * 8]
                    v.tensor_tensor(msl, msl, logits[:, qt * 8:(qt + 1) * 8], ALU.add)
                for qt in range(4):
                    v.tensor_reduce(v2c[:, qt:qt + 1],
                                    maskd[:, qt * 8:(qt + 1) * 8], AX.X, ALU.max)
                v.tensor_copy(junk0[:, 0:4], v2c[:, :])
                v.tensor_copy(junk0[:, 4:8], v2c[:, :])
                v.tensor_tensor(dcol[:, :], v2c[:, :], m1c[:, :],
                                ALU.subtract).then_inc(s_rt, 1)
            if want_d2:
                items.append((te_slot(19) + 0.4, f_rt1))

            def f_rt2():
                v.wait_ge(s_rt, 3 * qb + 2)
                v.tensor_scalar(e2col[:, :], ecol[:, :], 1.0, None, ALU.add)
                v.tensor_scalar(g2col[:, :], nmax[:, :], 1.5, None, ALU.is_le)
                v.tensor_copy(junk0[:, 0:4], e2col[:, 0:4])
                v.reciprocal(e3col[:, :], e2col[:, :])
                v.tensor_copy(junk0[:, 0:4], e3col[:, 0:4])
                v.tensor_copy(junk0[:, 4:8], e3col[:, 0:4])
                v.tensor_scalar(gcol[:, :], e3col[:, :], 1.0, None, ALU.is_ge)
                v.tensor_copy(junk0[:, 0:4], gcol[:, 0:4])
                v.tensor_copy(junk0[:, 4:8], gcol[:, 0:4])
                v.tensor_tensor(g3col[:, :], gcol[:, :], g2col[:, :], ALU.mult)
                v.tensor_copy(junk0[:, 0:4], g3col[:, 0:4])
                v.tensor_copy(junk0[:, 4:8], g3col[:, 0:4])
                for qt in range(4):
                    ins = v.tensor_scalar(prob_all[:, qt * 8:(qt + 1) * 8],
                                          ismax[:, qt * 8:(qt + 1) * 8],
                                          g3col[:, qt:qt + 1], None, ALU.mult)
                ins.then_inc(s_rt, 1)
            if want_d2:
                items.append((te_slot(23) + 0.6, f_rt2))
            for e in range(NE if want_e else 0):
                we = qb * NE + e
                base = 20 + e * 12
                for fc in range(FC):
                    def f(we=we, fc=fc):
                        hb = we * FC + fc
                        v.wait_ge(s_hmm, hb + 1)
                        v.tensor_scalar(r_bf[:, :], ps_P6[:, :], 0.0, None,
                                        ALU.max).then_inc(s_hrl, 1)
                        v.tensor_tensor(H_sb[:, fc * 512:(fc + 1) * 512],
                                        r_bf[:, :], r_bf[:, :],
                                        ALU.mult).then_inc(s_hsq, 1)
                    items.append((te_slot(base + fc) + 0.2, f))
                for blk in range(8):
                    def f(e=e, we=we, blk=blk, qb=qb):
                        ob = we * 8 + blk
                        tt, dmb = divmod(blk, 2)
                        v.wait_ge(s_omm, ob + 1)
                        sl = x1[:, tt * D + dmb * 512: tt * D + (dmb + 1) * 512]
                        if e < 8:
                            v.tensor_scalar(tmpO[:, :], ps_P7[:, :],
                                            prob_all[:, tt * 8 + e: tt * 8 + e + 1],
                                            None, ALU.mult)
                            ins = v.tensor_tensor(sl, tmpO[:, :], sl, ALU.add)
                        else:
                            ins = v.tensor_tensor(sl, ps_P7[:, :], sl, ALU.add)
                        ins.then_inc(s_oev, 1)
                    items.append((te_slot(base + 4 + blk) + 0.2, f))
            items.sort(key=lambda kv: kv[0])
            return items

        @block.tensor
        def _(te):
            te.wait_ge(s_ph, PH)

            def score(g, qbuf, kt_b):
                pg, kt = divmod(g, NKT)
                p = pg % PAIRS
                te.matmul(ps_S[g % 2][:, 0:512],
                          kt_b[0:64, kt * 128:(kt + 1) * 128],
                          qbuf[0:64, p * 512:(p + 1) * 512],
                          start=True, stop=True, tile_position=(0, 0))
                te.matmul(ps_S[g % 2][:, 512:1024],
                          kt_b[64:128, kt * 128:(kt + 1) * 128],
                          qbuf[64:128, p * 512:(p + 1) * 512],
                          start=True, stop=True,
                          tile_position=(64, 0)).then_inc(s_sc, 1)

            def av(g, v_b):
                pg, kt = divmod(g, NKT)
                if kt == 0:
                    te.wait_ge(s_nrm, pg)
                sl = P_buf[:, (g % 4) * 1024:(g % 4 + 1) * 1024]
                te.matmul(ps_OA[0:VS, :], v_b[:, 0, kt, 0:VS], sl[:, 0:512],
                          start=(kt == 0), stop=(kt == NKT - 1))
                te.matmul(ps_OB[0:VS, :], v_b[:, 1, kt, 0:VS], sl[:, 512:1024],
                          start=(kt == 0), stop=(kt == NKT - 1)).then_inc(s_av, 1)

            for qb in range(QBN):
                fill = pe_items(te, qb - 1) if (qb > 0 and STAGE != "B") else []
                fi = 0
                te.wait_ge(s_qld, 16 * (qb + 1))
                qbuf = QT_b[qb % 2]
                for p in range(PAIRS):
                    pg = qb * PAIRS + p
                    te.wait_ge(s_kvld, 32 * (pg + 1))
                    kt_b = KT_buf[pg % 2]
                    v_b = V_buf[pg % 2][:, :].rearrange(
                        "p (h kt c) -> p h kt c", h=2, kt=NKT)
                    for kt in range(NKT):
                        gg = pg * NKT + kt
                        if gg >= 2:
                            te.wait_ge(s_exp, gg - 1)
                        score(gg, qbuf, kt_b)
                        if kt > 0:
                            te.wait_ge(s_exp, gg)
                            av(gg - 1, v_b)
                        if kt % 4 == 3 and fi < len(fill):
                            fill[fi]()
                            fi += 1
                    te.wait_ge(s_exp, (pg + 1) * NKT)
                    av((pg + 1) * NKT - 1, v_b)
                while fi < len(fill):
                    fill[fi]()
                    fi += 1
            if STAGE != "B":
                for f in pe_items(te, QBN - 1):
                    f()
            if STAGE == "full":
                te.wait_ge(s_oev, NOB * QBN)
            elif STAGE == "BCD":
                te.wait_ge(s_rt, 12)
            elif STAGE in ("BC", "BCD1", "BCT", "BCSN"):
                te.wait_ge(s_cev, 32)
            else:
                te.wait_ge(s_nst, 32)
            te.sem_inc(s_ph, 1)

        @block.scalar
        def _(sc):
            sc.wait_ge(s_ph, PH)
            for qb in range(QBN):
                for gl in range(NGQ):
                    gg = qb * NGQ + gl
                    sc.wait_ge(s_sc, gg + 1)
                    if gg >= 4:
                        sc.wait_ge(s_av, gg - 3)
                    sc.activation(P_buf[:, (gg % 4) * 1024:(gg % 4 + 1) * 1024],
                                  ps_S[gg % 2][:, :], AF.Exp,
                                  bias=biasC[:, :], scale=0.125).then_inc(s_exp, 1)
                    if qb > 0 and STAGE in ("BCD", "BCD1", "full") and gl == 48:
                        sc.wait_ge(s_d2s, qb)
                        sc.activation(d2_lt[:, :], d2_v[:, :], AF.Ln,
                                      bias=epsc[:, :], scale=1.0)
                        sc.activation(d2_r[:, :], d2_lt[:, :], AF.Exp, bias=0.0,
                                      scale=-0.5).then_inc(s_dr, 1)
                    if qb > 0 and STAGE in ("BCD", "full") and gl == 88:
                        sc.wait_ge(s_rt, 3 * (qb - 1) + 1)
                        sc.activation(ecol[:, :], dcol[:, :], AF.Exp, bias=0.0,
                                      scale=1.0).then_inc(s_rt, 1)
            if STAGE in ("BCD", "BCD1", "full"):
                sc.wait_ge(s_d2s, QBN)
                sc.activation(d2_lt[:, :], d2_v[:, :], AF.Ln, bias=epsc[:, :], scale=1.0)
                sc.activation(d2_r[:, :], d2_lt[:, :], AF.Exp, bias=0.0,
                              scale=-0.5).then_inc(s_dr, 1)
            if STAGE in ("BCD", "full"):
                sc.wait_ge(s_rt, 3 * (QBN - 1) + 1)
                sc.activation(ecol[:, :], dcol[:, :], AF.Exp, bias=0.0,
                              scale=1.0).then_inc(s_rt, 1)
            sc.sem_inc(s_ph, 1)

        @block.vector
        def _(v):
            v.wait_ge(s_ph, PH)

            def nrm(pg):
                p = pg % PAIRS
                v.wait_ge(s_av, NKT * (pg + 1))
                v.tensor_copy(OTF32[0:VS, 0:512], ps_OA[0:VS, :])
                v.tensor_copy(OTF32[0:VS, 512:1024],
                              ps_OB[0:VS, :]).then_inc(s_nrm, 1)
                v.reciprocal(rinv[64:65, 0:512], OTF32[64:65, 0:512])
                v.reciprocal(rinv[64:65, 512:1024],
                             OTF32[64:65, 512:1024]).then_inc(s_riv, 1)
                v.wait_ge(s_rbc, 32 * (pg + 1))
                if pg >= 2:
                    v.wait_ge(s_ots, 32 * (pg - 1))
                v.tensor_tensor(OT_stage[pg % 2][:, 0:512],
                                OTF32[0:64, 0:512], rb_sb[:, 0:512], ALU.mult)
                v.tensor_tensor(OT_stage[pg % 2][:, 512:1024],
                                OTF32[0:64, 512:1024], rb_sb[:, 512:1024],
                                ALU.mult).then_inc(s_nst, 1)

            for qb in range(QBN):
                vf = v_items(v, qb - 1) if (qb > 0 and STAGE != "B") else []
                vi = 0
                for p in range(PAIRS):
                    pg = qb * PAIRS + p
                    lim = NKT * (p + 1)
                    while vi < len(vf) and vf[vi][0] < lim:
                        vf[vi][1]()
                        vi += 1
                    nrm(pg)
                while vi < len(vf):
                    vf[vi][1]()
                    vi += 1
            if STAGE != "B":
                for key, f in v_items(v, QBN - 1):
                    f()
            v.sem_inc(s_ph, 1)

        @block.sync
        def _(sync):
            sync.wait_ge(s_ph, PH)
            sync.dma_start(out=bobc[:, :],
                           in_=bass.AP(bo, 0, [[0, 128], [1, D]])).then_inc(s_bo, 16)
            for qb in range(QBN):
                if qb >= 2:
                    sync.wait_ge(s_sc, NGQ * (qb - 1))
                sync.dma_start(out=QT_b[qb % 2][:, :].rearrange(
                                   "p (pr c) -> p pr c", pr=PAIRS),
                               in_=QT_dram.ap()[:, :, qb * 512:(qb + 1) * 512],
                               ).then_inc(s_qld, 16)
                if qb > 0 and STAGE not in ("B", "BC", "BCSN"):
                    for ti in range(4):
                        sync.wait_ge(s_ln2, 4 * (qb - 1) + ti + 1)
                        sync.dma_start_transpose(
                            out=xn2[:, :, ti * 128:(ti + 1) * 128],
                            in_=xn2_t[ti % 2][:, :]).then_inc(s_xn2, 16)
                for p in range(PAIRS):
                    pg = qb * PAIRS + p
                    sync.wait_ge(s_riv, pg + 1)
                    sync.dma_start(out=rv_dram.ap()[pg % 2:pg % 2 + 1, :],
                                   in_=rinv[64:65, 0:1024]).then_inc(s_rbc, 16)
                    sync.wait_ge(s_rbc, 32 * pg + 16)
                    sync.dma_start(out=rb_sb[:, :],
                                   in_=bass.AP(rv_dram, (pg % 2) * D,
                                               [[0, 64], [1, 1024]]),
                                   ).then_inc(s_rbc, 16)
                    sync.wait_ge(s_nst, pg + 1)
                    sync.dma_start(out=OT_pack[qb % 2][0:64, p * 512:(p + 1) * 512],
                                   in_=OT_stage[pg % 2][:, 0:512]).then_inc(s_ots, 16)
                    sync.dma_start(out=OT_pack[qb % 2][64:128, p * 512:(p + 1) * 512],
                                   in_=OT_stage[pg % 2][:, 512:1024]).then_inc(s_ots, 16)
                if qb > 0:
                    for tt in range(4):
                        if STAGE == "full":
                            sync.wait_ge(s_oev, NOB * (qb - 1) + 66 + 2 * tt)
                        elif STAGE == "BCD":
                            sync.wait_ge(s_rt, 3 * qb)
                        elif STAGE in ("BC", "BCD1", "BCT", "BCSN"):
                            sync.wait_ge(s_cev, 8 * qb)
                        else:
                            sync.wait_ge(s_nst, 8 * qb)
                        sync.dma_start(out=out_r[:, (qb - 1) * 4 + tt, :],
                                       in_=x1r[:, tt, :]).then_inc(s_od, 16)
                sync.wait_ge(s_od, 64 * qb)
                sync.dma_start(
                    out=x1[:, :].rearrange("p (a d) -> p a d", a=4),
                    in_=xb.ap().rearrange("(a p) d -> p a d", p=128)[:, qb * 4:(qb + 1) * 4, :],
                ).then_inc(s_x1l, 16)
            if STAGE not in ("B", "BC", "BCSN"):
                for ti in range(4):
                    sync.wait_ge(s_ln2, 4 * (QBN - 1) + ti + 1)
                    sync.dma_start_transpose(
                        out=xn2[:, :, ti * 128:(ti + 1) * 128],
                        in_=xn2_t[ti % 2][:, :]).then_inc(s_xn2, 16)
            for tt in range(4):
                if STAGE == "full":
                    sync.wait_ge(s_oev, NOB * (QBN - 1) + 66 + 2 * tt)
                elif STAGE == "BCD":
                    sync.wait_ge(s_rt, 12)
                elif STAGE in ("BC", "BCD1", "BCT", "BCSN"):
                    sync.wait_ge(s_cev, 32)
                else:
                    sync.wait_ge(s_nst, 32)
                sync.dma_start(out=out_r[:, (QBN - 1) * 4 + tt, :],
                               in_=x1r[:, tt, :]).then_inc(s_od, 16)

        @block.gpsimd
        def _(gp):
            gp.wait_ge(s_ph, PH)
            gp.dma_start(
                out=WoT_sb[:, :].rearrange("p (pr d) -> p pr d", pr=PAIRS),
                in_=WoT.ap().rearrange("(pr p) d -> p pr d", p=128),
            ).then_inc(s_wc, 16)
            gp.dma_start(
                out=Wr_sb[:, :].rearrange("p (cc e) -> p cc e", cc=CC),
                in_=Wr.ap().rearrange("(cc p) e -> p cc e", p=128),
            ).then_inc(s_wc, 16)
            W1src = [W1.ap()[ei] for ei in range(8)] + [Ws1.ap()]
            W2src = [W2.ap()[ei] for ei in range(8)] + [Ws2.ap()]
            def wload(cq, e):
                we = cq * NE + e
                if cq == 0:
                    gp.wait_ge(s_stg, 32 * KB + 32 * (e + 1))
                if we >= 2:
                    gp.wait_ge(s_hmm, FC * (we - 1))
                gp.dma_start(out=W1_sb[we % 2][:, :],
                             in_=Wf1_dram.ap()[e]).then_inc(s_wld, 16)
                if we >= 1:
                    gp.wait_ge(s_omm, 8 * we)
                gp.dma_start(out=W2_sb[:, :],
                             in_=Wf2_dram.ap()[e]).then_inc(s_w2l, 16)

            for qb in range(QBN):
                for p in range(PAIRS):
                    pg = qb * PAIRS + p
                    if pg >= 2:
                        gp.wait_ge(s_av, NKT * (pg - 1))
                    gp.dma_start(out=KT_buf[pg % 2][:, :],
                                 in_=KT_dram.ap()[:, p, :]).then_inc(s_kvld, 16)
                    gp.dma_start(
                        out=V_buf[pg % 2][:, :].rearrange("p (h c) -> p h c", h=2),
                        in_=V_dram.ap()[:, 2 * p:2 * p + 2, :],
                    ).then_inc(s_kvld, 16)
                    if qb == 0:
                        for e in ([p] if p < 7 else [7, 8]):
                            gp.dma_start(
                                out=Wf1_dram.ap()[e].rearrange("p (cc f) -> p cc f", cc=CC),
                                in_=W1src[e].rearrange("(cc p) f -> p cc f", p=128),
                            ).then_inc(s_stg, 16)
                            gp.dma_start(
                                out=Wf2_dram.ap()[e].rearrange("p (fc d) -> p fc d", fc=FC),
                                in_=W2src[e].rearrange("(fc p) d -> p fc d", p=128),
                            ).then_inc(s_stg, 16)
                    elif STAGE == "full":
                        for e in ([p] if p < 7 else [7, 8]):
                            wload(qb - 1, e)
            if STAGE == "full":
                for e in range(NE):
                    wload(QBN - 1, e)
            gp.wait_ge(s_od, 64 * QBN)
            gp.sem_inc(s_ph, 1)

        L.close()
    return nc


_PROG_CACHE = {}


def _get_program():
    if "p" not in _PROG_CACHE:
        _PROG_CACHE["p"] = build_program()
    return _PROG_CACHE["p"]


def make_in_maps(x, WqT, WkT, WvT, WoT, bo, Ws1, Ws2, W1, W2, Wr):
    in_maps = []
    for core in range(8):
        bidx, qidx = divmod(core, 4)
        xb_c = np.ascontiguousarray(np.roll(x[bidx], -qidx * NQ, axis=0))
        in_maps.append({
            "xb": xb_c, "WqT": WqT, "WkT": WkT, "WvT": WvT, "WoT": WoT,
            "bo": bo, "Ws1": Ws1, "Ws2": Ws2, "W1": W1, "W2": W2, "Wr": Wr,
        })
    return in_maps


def prep_inputs(L0_attn_x, L0_attn_W_q, L0_attn_W_k, L0_attn_W_v, L0_attn_W_o,
                L0_attn_b_o, L0_ffn_Ws1, L0_ffn_Ws2, L0_ffn_W1, L0_ffn_W2,
                L0_ffn_Wr):
    f = lambda v: np.ascontiguousarray(np.asarray(v, dtype=np.float32))
    ft = lambda v: np.ascontiguousarray(np.asarray(v, dtype=np.float32).T)
    return (f(L0_attn_x), ft(L0_attn_W_q), ft(L0_attn_W_k), ft(L0_attn_W_v),
            ft(L0_attn_W_o), f(L0_attn_b_o), f(L0_ffn_Ws1), f(L0_ffn_Ws2),
            f(L0_ffn_W1), f(L0_ffn_W2), f(L0_ffn_Wr))


def kernel(**inputs):
    arrs = prep_inputs(**inputs)
    nc = _get_program()
    in_maps = make_in_maps(*arrs)
    res = run_bass_kernel_spmd(nc, in_maps, core_ids=list(range(8)))
    out = np.empty((2, LK, D), dtype=np.float32)
    for core in range(8):
        bidx, qidx = divmod(core, 4)
        out[bidx, qidx * NQ:(qidx + 1) * NQ] = np.asarray(
            res.results[core]["out"], dtype=np.float32)
    return out


# version 09 (final)
# speedup vs baseline: 1.0671x; 1.0671x over previous
import numpy as np
import ml_dtypes
from contextlib import ExitStack

import concourse.bass as bass
import concourse.mybir as mybir
from concourse.bass_utils import run_bass_kernel_spmd

F32 = mybir.dt.float32
BF16 = mybir.dt.bfloat16
AF = mybir.ActivationFunctionType
ALU = mybir.AluOpType
AX = mybir.AxisListType

D = 1024
LK = 8192
NQ = 2048
CC = 8
KB = 16
QKB = 4
PAIRS = 8
HEADS = 16
NKT = 64
CSH = 10.0
EPS = 1e-5
FC = 4
NE = 9
VS = 66
QBN = 4
QL = 512
NGQ = PAIRS * NKT
NPQ = QBN * PAIRS

LNA = 5 * KB
TPA = 8 * 4 * KB
XNA = 8 * 4 * KB
RIA = KB


STAGE = "full"


def build_program():
    nc = bass.Bass("TRN2", target_bir_lowering=False,
                   detect_race_conditions=False)

    xb = nc.dram_tensor("xb", [LK, D], F32, kind="ExternalInput")
    WqT = nc.dram_tensor("WqT", [D, D], F32, kind="ExternalInput")
    WkT = nc.dram_tensor("WkT", [D, D], F32, kind="ExternalInput")
    WvT = nc.dram_tensor("WvT", [D, D], F32, kind="ExternalInput")
    WoT = nc.dram_tensor("WoT", [D, D], F32, kind="ExternalInput")
    bo = nc.dram_tensor("bo", [D], F32, kind="ExternalInput")
    Ws1 = nc.dram_tensor("Ws1", [D, 512], F32, kind="ExternalInput")
    Ws2 = nc.dram_tensor("Ws2", [512, D], F32, kind="ExternalInput")
    W1 = nc.dram_tensor("W1", [8, D, 512], F32, kind="ExternalInput")
    W2 = nc.dram_tensor("W2", [8, 512, D], F32, kind="ExternalInput")
    Wr = nc.dram_tensor("Wr", [D, 8], F32, kind="ExternalInput")
    out_d = nc.dram_tensor("out", [NQ, D], F32, kind="ExternalOutput")

    KT_dram = nc.dram_tensor("KT_scr", [128, PAIRS, LK], BF16)
    V_dram = nc.dram_tensor("V_scr", [128, HEADS, NKT * VS], BF16)
    QT_dram = nc.dram_tensor("QT_scr", [128, PAIRS, NQ], BF16)
    rv_dram = nc.dram_tensor("rv_scr", [2, D], F32)
    Wf1_dram = nc.dram_tensor("Wf1_scr", [NE, 128, CC * 512], BF16)
    Wf2_dram = nc.dram_tensor("Wf2_scr", [NE, 128, FC * D], BF16)

    ident_dram = nc.inline_tensor(np.eye(128, dtype=ml_dtypes.bfloat16), "ident_c")

    g = ExitStack()
    with g:
        sem = lambda n: g.enter_context(nc.semaphore(n))

        ident = g.enter_context(nc.sbuf_tensor("ident", [128, 128], BF16))
        biasC = g.enter_context(nc.sbuf_tensor("biasC", [128, 1], F32))
        epsc = g.enter_context(nc.sbuf_tensor("epsc", [128, 1], F32))
        junk0 = g.enter_context(nc.sbuf_tensor("junk0", [128, 8], F32))
        scr2 = g.enter_context(nc.sbuf_tensor("scr2", [128, D], F32))

        s_ph = sem("s_ph")
        s_x = sem("s_x")
        s_wc = sem("s_wc")
        s_ln = sem("s_ln")
        s_tp = sem("s_tp")
        s_xnb = sem("s_xnb")
        s_kmm = sem("s_kmm")
        s_kev = sem("s_kev")
        s_stg = sem("s_stg")
        s_rinv = sem("s_rinv")
        s_qld = sem("s_qld")
        s_kvld = sem("s_kvld")
        s_sc = sem("s_sc")
        s_exp = sem("s_exp")
        s_av = sem("s_av")
        s_nrm = sem("s_nrm")
        s_riv = sem("s_riv")
        s_rbc = sem("s_rbc")
        s_nst = sem("s_nst")
        s_ots = sem("s_ots")
        s_cmm = sem("s_cmm")
        s_cbo = sem("s_cbo")
        s_cev = sem("s_cev")
        s_x1l = sem("s_x1l")
        s_d2s = sem("s_d2s")
        s_dr = sem("s_dr")
        s_ln2 = sem("s_ln2")
        s_tp2 = sem("s_tp2")
        s_xn2 = sem("s_xn2")
        s_lg = sem("s_lg")
        s_lge = sem("s_lge")
        s_rt = sem("s_rt")
        s_wld = sem("s_wld")
        s_w2l = sem("s_w2l")
        s_hmm = sem("s_hmm")
        s_hrl = sem("s_hrl")
        s_hsq = sem("s_hsq")
        s_omm = sem("s_omm")
        s_oev = sem("s_oev")
        s_od = sem("s_od")
        s_qst = sem("s_qst")
        s_bo = sem("s_bo")

        block = g.enter_context(nc.Block())

        a = ExitStack()
        asb = lambda n, s, d: a.enter_context(nc.sbuf_tensor(n, s, d))
        aps = lambda n, s, d: a.enter_context(nc.psum_tensor(n, s, d))
        WqT_sb = asb("WqT_sb", [128, CC * D], BF16)
        WkT_sb = asb("WkT_sb", [128, CC * D], BF16)
        WvT_sb = asb("WvT_sb", [128, CC * D], BF16)
        xblk = [asb(f"xblk{i}", [128, 4 * D], F32) for i in range(2)]
        xn_t = [asb(f"xn_t{i}", [128, D], BF16) for i in range(4)]
        scrq = asb("scrq", [128, 4 * D], F32)
        st_s1 = asb("st_s1", [128, 4], F32)
        st_s2 = asb("st_s2", [128, 4], F32)
        st_m = asb("st_m", [128, 4], F32)
        st_q = asb("st_q", [128, 4], F32)
        st_lt = asb("st_lt", [128, 4], F32)
        st_t = asb("st_t", [128, 4], F32)
        st_v = asb("st_v", [128, 4], F32)
        st_r = asb("st_r", [128, 4], F32)
        xnT = [asb(f"xnT{i}", [128, CC * 512], BF16) for i in range(2)]
        KT_stage = [asb(f"KT_st{i}", [128, PAIRS * 512], BF16) for i in range(2)]
        V_stage = [asb(f"V_st{i}", [128, HEADS * 4 * VS], BF16) for i in range(2)]
        QT_stage = [asb(f"QT_st{i}", [128, PAIRS * 512], BF16) for i in range(2)]
        ps_T = [aps(f"ps_T{i}", [128, 128], BF16) for i in range(4)]
        ps_M = [aps(f"ps_M{i}", [128, 512], F32) for i in range(4)]

        kvq_sched = []
        for kb in range(KB):
            for pr in range(PAIRS):
                kvq_sched.append((kb, "K", pr))
            for kt in range(4):
                for half in range(2):
                    kvq_sched.append((kb, "V", (kt, half)))
            if kb < QKB:
                for pr in range(PAIRS):
                    kvq_sched.append((kb, "Q", pr))
        kb_end_count = {}
        for i, (kb, kind, argv) in enumerate(kvq_sched):
            kb_end_count[kb] = i + 1
        kb_start = {kb: (kb_end_count[kb - 1] if kb else 0) for kb in range(KB)}

        @block.sync
        def _(sync):
            sync.dma_start(out=ident[:, :], in_=ident_dram.ap()).then_inc(s_x, 16)
            for kb in range(KB):
                if kb >= 2:
                    sync.wait_ge(s_ln, 5 * (kb - 1))
                sync.dma_start(
                    out=xblk[kb % 2][:, :].rearrange("p (s d) -> p s d", s=4),
                    in_=xb.ap().rearrange("(kb s p) d -> kb p s d", p=128, s=4)[kb],
                ).then_inc(s_x, 16)
            sync.sem_inc(s_ph, 1)

        @block.gpsimd
        def _(gp):
            for W, Wsb in ((WqT, WqT_sb), (WkT, WkT_sb), (WvT, WvT_sb)):
                gp.dma_start(
                    out=Wsb[:, :].rearrange("p (cc d) -> p cc d", cc=CC),
                    in_=W.ap().rearrange("(cc p) d -> p cc d", p=128),
                ).then_inc(s_wc, 16)
            for i in range(2):
                vst = V_stage[i][:, :].rearrange("p (h kt c) -> p h kt c", h=HEADS, kt=4)
                gp.memset(vst[:, :, :, 64:65], 1.0)
                gp.memset(vst[:, :, :, 65:66], 0.0)
            for kb in range(KB):
                gp.wait_ge(s_kev, kb_end_count[kb])
                gp.dma_start(
                    out=KT_dram.ap()[:, :, kb * 512:(kb + 1) * 512],
                    in_=KT_stage[kb % 2][:, :].rearrange("p (pr c) -> p pr c", pr=PAIRS),
                ).then_inc(s_stg, 16)
                gp.dma_start(
                    out=V_dram.ap()[:, :, kb * 4 * VS:(kb + 1) * 4 * VS],
                    in_=V_stage[kb % 2][:, :].rearrange("p (h c) -> p h c", h=HEADS),
                ).then_inc(s_stg, 16)
                if kb < QKB:
                    gp.dma_start(
                        out=QT_dram.ap()[:, :, kb * 512:(kb + 1) * 512],
                        in_=QT_stage[kb % 2][:, :].rearrange("p (pr c) -> p pr c", pr=PAIRS),
                    ).then_inc(s_qst, 16)
            gp.wait_ge(s_stg, 32 * KB)
            gp.wait_ge(s_qst, 16 * QKB)
            gp.sem_inc(s_ph, 1)

        @block.vector
        def _(v):
            v.memset(biasC[:, :], -CSH)
            v.memset(epsc[:, :], EPS)
            v.memset(scrq[:, :], 1.0)
            v.memset(junk0[:, :], 1.0)
            for _w in range(3):
                v.reduce_sum(junk0[:, 0:1], scrq[:, 0:D], axis=AX.X)
                v.tensor_tensor(scrq[:, 0:D], scrq[:, 0:D], scrq[:, 0:D], ALU.mult)
                v.tensor_tensor(junk0[:, 4:8], junk0[:, 0:4], junk0[:, 0:4],
                                ALU.subtract)
                v.tensor_scalar(junk0[:, 0:4], junk0[:, 4:8], 0.5, None, ALU.mult)
                v.tensor_copy(junk0[:, 4:8], junk0[:, 0:4])
                v.reciprocal(junk0[:, 0:4], scrq[:, 0:4])
            for kb in range(KB):
                v.wait_ge(s_x, 16 * (kb + 2))
                for t in range(4):
                    xt = xblk[kb % 2][:, t * D:(t + 1) * D]
                    sq = scrq[:, t * D:(t + 1) * D]
                    v.reduce_sum(st_s1[:, t:t + 1], xt, axis=AX.X)
                    v.tensor_tensor(sq, xt, xt, ALU.mult)
                    v.reduce_sum(st_s2[:, t:t + 1], sq, axis=AX.X)
                v.tensor_scalar(st_m[:, :], st_s1[:, :], 1.0 / D, None, ALU.mult)
                v.tensor_scalar(st_q[:, :], st_s2[:, :], 1.0 / D, None, ALU.mult)
                v.tensor_copy(junk0[:, 0:4], st_m[:, :])
                v.tensor_copy(junk0[:, 4:8], st_m[:, :])
                v.tensor_tensor(st_t[:, :], st_m[:, :], st_m[:, :], ALU.mult)
                v.tensor_copy(junk0[:, 0:4], st_t[:, :])
                v.tensor_copy(junk0[:, 4:8], st_t[:, :])
                v.tensor_tensor(st_v[:, :], st_q[:, :], st_t[:, :],
                                ALU.subtract).then_inc(s_ln, 1)
                v.wait_ge(s_rinv, kb + 1)
                for t in range(4):
                    T = 4 * kb + t
                    if T >= 4:
                        v.wait_ge(s_tp, 8 * (T - 3))
                    xt = xblk[kb % 2][:, t * D:(t + 1) * D]
                    v.tensor_scalar(xn_t[t][:, :], xt, st_m[:, t:t + 1],
                                    st_r[:, t:t + 1], ALU.subtract,
                                    ALU.mult).then_inc(s_ln, 1)
                for t in range(4):
                    T = 4 * kb + t
                    if kb >= 2 and t == 0:
                        v.wait_ge(s_kmm, kb_end_count[kb - 2])
                    dst = xnT[kb % 2][:, :].rearrange("p (cc c) -> p cc c", cc=CC)
                    for c in range(CC):
                        gt = T * 8 + c
                        v.wait_ge(s_tp, gt + 1)
                        v.tensor_copy(dst[:, c, t * 128:(t + 1) * 128],
                                      ps_T[gt % 4][:, :]).then_inc(s_xnb, 1)
            v.sem_inc(s_ph, 1)

        @block.scalar
        def _(sc):
            gidx = 0
            for kb in range(KB):
                sc.wait_ge(s_ln, 5 * kb + 1)
                sc.activation(st_lt[:, :], st_v[:, :], AF.Ln, bias=epsc[:, :], scale=1.0)
                sc.activation(st_r[:, :], st_lt[:, :], AF.Exp, bias=0.0,
                              scale=-0.5).then_inc(s_rinv, 1)
                q_seen = False
                for _ in range(kb_end_count[kb] - kb_start[kb]):
                    kbx, kind, argv = kvq_sched[gidx]
                    sc.wait_ge(s_kmm, gidx + 1)
                    if kb >= 2 and gidx == kb_start[kb]:
                        sc.wait_ge(s_stg, 32 * (kb - 1))
                    src = ps_M[gidx % 4]
                    if kind == "K":
                        sc.copy(KT_stage[kb % 2][:, argv * 512:(argv + 1) * 512],
                                src[:, :]).then_inc(s_kev, 1)
                    elif kind == "V":
                        kt, half = argv
                        dst = V_stage[kb % 2][:, :].rearrange(
                            "p (h c) -> p h c", h=HEADS)
                        sc.copy(
                            dst[:, half * 8:(half + 1) * 8, kt * VS:kt * VS + 64],
                            src[:, :].rearrange("p (h c) -> p h c", h=8),
                        ).then_inc(s_kev, 1)
                    else:
                        if not q_seen and kb >= 2:
                            sc.wait_ge(s_qst, 16 * (kb - 1))
                        q_seen = True
                        sc.copy(QT_stage[kb % 2][:, argv * 512:(argv + 1) * 512],
                                src[:, :]).then_inc(s_kev, 1)
                    gidx += 1
            sc.sem_inc(s_ph, 1)

        @block.tensor
        def _(te):
            te.wait_ge(s_x, 16)
            te.wait_ge(s_wc, 48)
            gidx = 0
            for kb in range(KB):
                for t in range(4):
                    T = 4 * kb + t
                    te.wait_ge(s_ln, 5 * kb + t + 2)
                    for c in range(CC):
                        gt = T * 8 + c
                        if gt >= 4:
                            te.wait_ge(s_xnb, gt - 3)
                        te.transpose(ps_T[gt % 4][:, :],
                                     xn_t[t][:, c * 128:(c + 1) * 128],
                                     ident[:, :]).then_inc(s_tp, 1)
                te.wait_ge(s_xnb, 32 * (kb + 1))
                xn = xnT[kb % 2][:, :].rearrange("p (cc c) -> p cc c", cc=CC)
                for _ in range(kb_end_count[kb] - kb_start[kb]):
                    kbx, kind, argv = kvq_sched[gidx]
                    if gidx >= 4:
                        te.wait_ge(s_kev, gidx - 3)
                    dstp = ps_M[gidx % 4]
                    for c in range(CC):
                        if kind == "K":
                            ins = te.matmul(dstp[:, :],
                                            WkT_sb[:, c * D + argv * 128: c * D + (argv + 1) * 128],
                                            xn[:, c, :], start=(c == 0), stop=(c == CC - 1))
                        elif kind == "V":
                            kt, half = argv
                            ins = te.matmul(dstp[:, :],
                                            xn[:, c, kt * 128:(kt + 1) * 128],
                                            WvT_sb[:, c * D + half * 512: c * D + (half + 1) * 512],
                                            start=(c == 0), stop=(c == CC - 1))
                        else:
                            ins = te.matmul(dstp[:, :],
                                            WqT_sb[:, c * D + argv * 128: c * D + (argv + 1) * 128],
                                            xn[:, c, :], start=(c == 0), stop=(c == CC - 1))
                        if c == CC - 1:
                            ins.then_inc(s_kmm, 1)
                    gidx += 1
            te.wait_ge(s_kev, len(kvq_sched))
            te.sem_inc(s_ph, 1)

        PH = 5

        a.close()
        L = ExitStack()
        lsb = lambda n, sh, d: L.enter_context(nc.sbuf_tensor(n, sh, d))
        lps = lambda n, sh, d: L.enter_context(nc.psum_tensor(n, sh, d))

        QT_b = [lsb(f"QT_b{i}", [128, PAIRS * 512], BF16) for i in range(2)]
        KT_buf = [lsb(f"KT_b{i}", [128, LK], BF16) for i in range(2)]
        V_buf = [lsb(f"V_b{i}", [128, 2 * NKT * VS], BF16) for i in range(2)]
        P_buf = lsb("P_b", [128, 4 * 1024], BF16)
        OT_stage = [lsb(f"OT_st{i}", [64, 1024], BF16) for i in range(2)]
        OTF32 = lsb("OTF32", [VS, 1024], F32)
        rinv = lsb("rinv", [65, 1024], F32)
        rb_sb = lsb("rb_sb", [64, 1024], F32)
        OT_pack = [lsb(f"OT_pk{i}", [128, PAIRS * 512], BF16) for i in range(2)]
        WoT_sb = lsb("WoT_sb", [128, PAIRS * D], BF16)
        bobc = lsb("bobc", [128, D], F32)
        x1 = lsb("x1", [128, 4 * D], F32)
        xn2_t = [lsb(f"xn2_t{i}", [128, D], BF16) for i in range(2)]
        xn2T = lsb("xn2T", [128, CC * 512], BF16)
        d2_s1 = lsb("d2_s1", [128, 4], F32)
        d2_s2 = lsb("d2_s2", [128, 4], F32)
        d2_m = lsb("d2_m", [128, 4], F32)
        d2_q = lsb("d2_q", [128, 4], F32)
        d2_lt = lsb("d2_lt", [128, 4], F32)
        d2_t = lsb("d2_t", [128, 4], F32)
        d2_v = lsb("d2_v", [128, 4], F32)
        d2_r = lsb("d2_r", [128, 4], F32)
        Wr_sb = lsb("Wr_sb", [128, CC * 8], BF16)
        logits = lsb("logits", [128, 32], F32)
        ismax = lsb("ismax", [128, 32], F32)
        maskd = lsb("maskd", [128, 32], F32)
        nmax = lsb("nmax", [128, 4], F32)
        m1c = lsb("m1c", [128, 4], F32)
        v2c = lsb("v2c", [128, 4], F32)
        dcol = lsb("dcol", [128, 4], F32)
        ecol = lsb("ecol", [128, 4], F32)
        e2col = lsb("e2col", [128, 4], F32)
        e3col = lsb("e3col", [128, 4], F32)
        gcol = lsb("gcol", [128, 4], F32)
        g2col = lsb("g2col", [128, 4], F32)
        g3col = lsb("g3col", [128, 4], F32)
        prob_all = lsb("prob_all", [128, 32], F32)
        W1_sb = [lsb(f"W1_sb{i}", [128, CC * 512], BF16) for i in range(2)]
        W2_sb = lsb("W2_sb", [128, FC * D], BF16)
        H_sb = lsb("H_sb", [128, FC * 512], BF16)
        r_bf = lsb("r_bf", [128, 512], BF16)
        tmpO = lsb("tmpO", [128, 512], F32)

        ps_S = [lps(f"ps_S{i}", [128, 1024], F32) for i in range(2)]
        ps_OA = lps("ps_OA", [128, 512], F32)
        ps_OB = lps("ps_OB", [128, 512], F32)
        ps_P6 = lps("ps_P6", [128, 512], F32)
        ps_P7 = lps("ps_P7", [128, 512], F32)

        out_r = out_d.ap().rearrange("(tt p) d -> p tt d", p=128)
        x1r = x1[:, :].rearrange("p (qt d) -> p qt d", qt=4)
        xn2 = xn2T[:, :].rearrange("p (cc c) -> p cc c", cc=CC)

        NTE = 128
        NHB = NE * FC
        NOB = NE * 8

        def te_slot(i):
            return 4 * i + 3.5

        def pe_items(te, qb):
            items = []
            for _ in range(4):
                items.append(lambda: None)
            for blk in range(8):
                def f(blk=blk):
                    qt, dmb = divmod(blk, 2)
                    if blk == 0:
                        if STAGE == "full":
                            te.wait_ge(s_oev, NOB * qb)
                        te.wait_ge(s_ots, 32 * 8 * (qb + 1))
                        te.wait_ge(s_wc, 80)
                    if blk >= 2:
                        te.wait_ge(s_cev, 8 * qb + blk - 1)
                    dstp = ps_P6 if blk % 2 == 0 else ps_P7
                    for pr in range(PAIRS):
                        ins = te.matmul(dstp[:, :],
                                        OT_pack[qb % 2][:, pr * 512 + qt * 128: pr * 512 + (qt + 1) * 128],
                                        WoT_sb[:, pr * D + dmb * 512: pr * D + (dmb + 1) * 512],
                                        start=(pr == 0), stop=(pr == PAIRS - 1))
                    ins.then_inc(s_cmm, 1)
                items.append(f)
            for b in range(4):
                items.append(lambda: None)
            for qt in range(4):
                def f(qt=qt):
                    if qt == 0:
                        te.wait_ge(s_cev, 8 * (qb + 1))
                        te.wait_ge(s_wc, 80)
                    if qt >= 2:
                        te.wait_ge(s_lge, 4 * qb + qt - 1)
                    te.wait_ge(s_xn2, 64 * qb + 16 * (qt + 1))
                    dst = ps_P7[:, (qt % 2) * 16:(qt % 2) * 16 + 8]
                    for cc in range(CC):
                        ins = te.matmul(dst,
                                        xn2[:, cc, qt * 128:(qt + 1) * 128],
                                        Wr_sb[:, cc * 8:(cc + 1) * 8],
                                        start=(cc == 0), stop=(cc == CC - 1))
                    ins.then_inc(s_lg, 1)
                items.append(f)
            for e in range(NE):
                we = qb * NE + e
                for fc in range(FC):
                    def f(e=e, we=we, fc=fc):
                        hb = we * FC + fc
                        dstp = ps_P6 if fc % 2 == 0 else ps_P7
                        if fc == 0:
                            te.wait_ge(s_wld, 16 * (we + 1))
                            te.wait_ge(s_xn2, 64 * (qb + 1))
                            if e == 0:
                                te.wait_ge(s_cev, 8 * qb + 7)
                            else:
                                te.wait_ge(s_hsq, FC * we)
                                te.wait_ge(s_oev, 8 * (we - 1) + 7)
                        if fc == 1:
                            if e == 0:
                                te.wait_ge(s_lge, 4 * (qb + 1))
                            else:
                                te.wait_ge(s_oev, 8 * we)
                        if fc >= 2:
                            te.wait_ge(s_hrl, hb - 1)
                        for cc in range(CC):
                            ins = te.matmul(dstp[:, :],
                                            W1_sb[we % 2][:, cc * 512 + fc * 128:
                                                          cc * 512 + (fc + 1) * 128],
                                            xn2[:, cc, :],
                                            start=(cc == 0), stop=(cc == CC - 1))
                        ins.then_inc(s_hmm, 1)
                    items.append(f)
                for blk in range(8):
                    def f(e=e, we=we, blk=blk):
                        ob = we * 8 + blk
                        tt, dmb = divmod(blk, 2)
                        dstp = ps_P6 if blk % 2 == 0 else ps_P7
                        if blk == 0:
                            te.wait_ge(s_hsq, FC * (we + 1))
                            te.wait_ge(s_w2l, 16 * (we + 1))
                        if blk >= 2:
                            te.wait_ge(s_oev, ob - 1)
                        for fc in range(FC):
                            ins = te.matmul(dstp[:, :],
                                            H_sb[:, fc * 512 + tt * 128:
                                                 fc * 512 + (tt + 1) * 128],
                                            W2_sb[:, fc * D + dmb * 512:
                                                  fc * D + (dmb + 1) * 512],
                                            start=(fc == 0), stop=(fc == FC - 1))
                        ins.then_inc(s_omm, 1)
                    items.append(f)
            assert len(items) == NTE, len(items)
            if STAGE in ("BC", "BCSN"):
                items = [it if 4 <= i < 12 else (lambda: None)
                         for i, it in enumerate(items)]
            elif STAGE == "BCD":
                items = [it if 4 <= i < 20 else (lambda: None)
                         for i, it in enumerate(items)]
            elif STAGE in ("BCD1", "BCT"):
                items = [it if 4 <= i < 16 else (lambda: None)
                         for i, it in enumerate(items)]
            return items

        def v_items(v, qb):
            items = []
            want_d = STAGE in ("BCD", "BCD1", "BCT", "BCSN", "full")
            want_tc = STAGE != "BCSN"
            want_d2 = STAGE in ("BCD", "full")
            want_e = STAGE == "full"
            for qt in range(4):
                def f(qt=qt):
                    if qt == 0:
                        v.wait_ge(s_x1l, 16 * (qb + 1))
                        v.wait_ge(s_bo, 16)
                    v.tensor_tensor(x1r[:, qt, :], x1r[:, qt, :], bobc[:, :],
                                    ALU.add).then_inc(s_cbo, 1)
                items.append((te_slot(4) - 1.0 + qt * 0.01, f))
            for blk in range(8):
                def f(blk=blk):
                    qt, dmb = divmod(blk, 2)
                    v.wait_ge(s_cmm, 8 * qb + blk + 1)
                    srcp = ps_P6 if blk % 2 == 0 else ps_P7
                    sl = x1[:, qt * D + dmb * 512: qt * D + (dmb + 1) * 512]
                    v.tensor_tensor(sl, srcp[:, :], sl, ALU.add).then_inc(s_cev, 1)
                items.append((te_slot(4 + blk) + 0.2, f))
            for t in range(4 if want_d else 0):
                def f(t=t):
                    xt = x1r[:, t, :]
                    sq = scr2[:, :]
                    v.reduce_sum(d2_s1[:, t:t + 1], xt, axis=AX.X)
                    v.tensor_tensor(sq, xt, xt, ALU.mult)
                    v.reduce_sum(d2_s2[:, t:t + 1], sq, axis=AX.X)
                items.append((te_slot(5 + 2 * t) + 0.4, f))

            def f_stats():
                v.tensor_scalar(d2_m[:, :], d2_s1[:, :], 1.0 / D, None, ALU.mult)
                v.tensor_scalar(d2_q[:, :], d2_s2[:, :], 1.0 / D, None, ALU.mult)
                v.tensor_copy(junk0[:, 0:4], d2_m[:, :])
                v.tensor_copy(junk0[:, 4:8], d2_m[:, :])
                v.tensor_tensor(d2_t[:, :], d2_m[:, :], d2_m[:, :], ALU.mult)
                v.tensor_copy(junk0[:, 0:4], d2_t[:, :])
                v.tensor_copy(junk0[:, 4:8], d2_t[:, :])
                v.tensor_tensor(d2_v[:, :], d2_q[:, :], d2_t[:, :],
                                ALU.subtract).then_inc(s_d2s, 1)
            if want_d:
                items.append((te_slot(11) + 0.5, f_stats))
            for ti in range(4 if want_d else 0):
                def f(ti=ti):
                    if ti == 0 and STAGE not in ("BCT", "BCSN"):
                        v.wait_ge(s_dr, qb + 1)
                    if ti >= 2:
                        v.wait_ge(s_xn2, 64 * qb + 16 * (ti - 1))
                    v.tensor_scalar(xn2_t[ti % 2][:, :], x1r[:, ti, :],
                                    d2_m[:, ti:ti + 1], d2_r[:, ti:ti + 1],
                                    ALU.subtract, ALU.mult).then_inc(s_ln2, 1)
                items.append((te_slot(12 + ti) - 0.5, f))

            for qt in range(4 if want_d2 else 0):
                def f(qt=qt):
                    v.wait_ge(s_lg, 4 * qb + qt + 1)
                    v.tensor_copy(logits[:, qt * 8:(qt + 1) * 8],
                                  ps_P7[:, (qt % 2) * 16:(qt % 2) * 16 + 8],
                                  ).then_inc(s_lge, 1)
                items.append((te_slot(16 + qt) + 0.2, f))

            def f_rt1():
                for qt in range(4):
                    v.tensor_reduce(m1c[:, qt:qt + 1],
                                    logits[:, qt * 8:(qt + 1) * 8], AX.X, ALU.max)
                v.tensor_copy(junk0[:, 0:4], m1c[:, :])
                v.tensor_copy(junk0[:, 4:8], m1c[:, :])
                for qt in range(4):
                    v.tensor_scalar(ismax[:, qt * 8:(qt + 1) * 8],
                                    logits[:, qt * 8:(qt + 1) * 8],
                                    m1c[:, qt:qt + 1], None, ALU.is_equal)
                for qt in range(4):
                    v.tensor_reduce(nmax[:, qt:qt + 1],
                                    ismax[:, qt * 8:(qt + 1) * 8], AX.X, ALU.add)
                for qt in range(4):
                    v.tensor_scalar(maskd[:, qt * 8:(qt + 1) * 8],
                                    ismax[:, qt * 8:(qt + 1) * 8], -1e30, None,
                                    ALU.mult)
                for qt in range(4):
                    msl = maskd[:, qt * 8:(qt + 1) * 8]
                    v.tensor_tensor(msl, msl, logits[:, qt * 8:(qt + 1) * 8], ALU.add)
                for qt in range(4):
                    v.tensor_reduce(v2c[:, qt:qt + 1],
                                    maskd[:, qt * 8:(qt + 1) * 8], AX.X, ALU.max)
                v.tensor_copy(junk0[:, 0:4], v2c[:, :])
                v.tensor_copy(junk0[:, 4:8], v2c[:, :])
                v.tensor_tensor(dcol[:, :], v2c[:, :], m1c[:, :],
                                ALU.subtract).then_inc(s_rt, 1)
            if want_d2:
                items.append((te_slot(19) + 0.4, f_rt1))

            def f_rt2():
                v.wait_ge(s_rt, 3 * qb + 2)
                v.tensor_scalar(e2col[:, :], ecol[:, :], 1.0, None, ALU.add)
                v.tensor_scalar(g2col[:, :], nmax[:, :], 1.5, None, ALU.is_le)
                v.tensor_copy(junk0[:, 0:4], e2col[:, 0:4])
                v.reciprocal(e3col[:, :], e2col[:, :])
                v.tensor_copy(junk0[:, 0:4], e3col[:, 0:4])
                v.tensor_copy(junk0[:, 4:8], e3col[:, 0:4])
                v.tensor_scalar(gcol[:, :], e3col[:, :], 1.0, None, ALU.is_ge)
                v.tensor_copy(junk0[:, 0:4], gcol[:, 0:4])
                v.tensor_copy(junk0[:, 4:8], gcol[:, 0:4])
                v.tensor_tensor(g3col[:, :], gcol[:, :], g2col[:, :], ALU.mult)
                v.tensor_copy(junk0[:, 0:4], g3col[:, 0:4])
                v.tensor_copy(junk0[:, 4:8], g3col[:, 0:4])
                for qt in range(4):
                    ins = v.tensor_scalar(prob_all[:, qt * 8:(qt + 1) * 8],
                                          ismax[:, qt * 8:(qt + 1) * 8],
                                          g3col[:, qt:qt + 1], None, ALU.mult)
                ins.then_inc(s_rt, 1)
            if want_d2:
                items.append((te_slot(23) + 0.6, f_rt2))
            for e in range(NE if want_e else 0):
                we = qb * NE + e
                base = 20 + e * 12
                for fc in range(FC):
                    def f(we=we, fc=fc):
                        hb = we * FC + fc
                        srcp = ps_P6 if fc % 2 == 0 else ps_P7
                        v.wait_ge(s_hmm, hb + 1)
                        v.tensor_scalar(r_bf[:, :], srcp[:, :], 0.0, None,
                                        ALU.max).then_inc(s_hrl, 1)
                        v.tensor_tensor(H_sb[:, fc * 512:(fc + 1) * 512],
                                        r_bf[:, :], r_bf[:, :],
                                        ALU.mult).then_inc(s_hsq, 1)
                    items.append((te_slot(base + fc) + 0.2, f))
                for blk in range(8):
                    def f(e=e, we=we, blk=blk, qb=qb):
                        ob = we * 8 + blk
                        tt, dmb = divmod(blk, 2)
                        srcp = ps_P6 if blk % 2 == 0 else ps_P7
                        v.wait_ge(s_omm, ob + 1)
                        sl = x1[:, tt * D + dmb * 512: tt * D + (dmb + 1) * 512]
                        if e < 8:
                            v.tensor_scalar(tmpO[:, :], srcp[:, :],
                                            prob_all[:, tt * 8 + e: tt * 8 + e + 1],
                                            None, ALU.mult)
                            ins = v.tensor_tensor(sl, tmpO[:, :], sl, ALU.add)
                        else:
                            ins = v.tensor_tensor(sl, srcp[:, :], sl, ALU.add)
                        ins.then_inc(s_oev, 1)
                    items.append((te_slot(base + 4 + blk) + 0.2, f))
            items.sort(key=lambda kv: kv[0])
            return items

        @block.tensor
        def _(te):
            te.wait_ge(s_ph, PH)

            def score(g, qbuf, kt_b):
                pg, kt = divmod(g, NKT)
                p = pg % PAIRS
                te.matmul(ps_S[g % 2][:, 0:512],
                          kt_b[0:64, kt * 128:(kt + 1) * 128],
                          qbuf[0:64, p * 512:(p + 1) * 512],
                          start=True, stop=True, tile_position=(0, 0))
                te.matmul(ps_S[g % 2][:, 512:1024],
                          kt_b[64:128, kt * 128:(kt + 1) * 128],
                          qbuf[64:128, p * 512:(p + 1) * 512],
                          start=True, stop=True,
                          tile_position=(64, 0)).then_inc(s_sc, 1)

            def av(g, v_b):
                pg, kt = divmod(g, NKT)
                if kt == 0:
                    te.wait_ge(s_nrm, pg)
                sl = P_buf[:, (g % 4) * 1024:(g % 4 + 1) * 1024]
                te.matmul(ps_OA[0:VS, :], v_b[:, 0, kt, 0:VS], sl[:, 0:512],
                          start=(kt == 0), stop=(kt == NKT - 1))
                te.matmul(ps_OB[0:VS, :], v_b[:, 1, kt, 0:VS], sl[:, 512:1024],
                          start=(kt == 0), stop=(kt == NKT - 1)).then_inc(s_av, 1)

            for qb in range(QBN):
                fill = pe_items(te, qb - 1) if (qb > 0 and STAGE != "B") else []
                fi = 0
                te.wait_ge(s_qld, 16 * (qb + 1))
                qbuf = QT_b[qb % 2]
                for p in range(PAIRS):
                    pg = qb * PAIRS + p
                    te.wait_ge(s_kvld, 32 * (pg + 1))
                    kt_b = KT_buf[pg % 2]
                    v_b = V_buf[pg % 2][:, :].rearrange(
                        "p (h kt c) -> p h kt c", h=2, kt=NKT)
                    for kt in range(NKT):
                        gg = pg * NKT + kt
                        if gg >= 2:
                            te.wait_ge(s_exp, gg - 1)
                        score(gg, qbuf, kt_b)
                        if kt > 0:
                            te.wait_ge(s_exp, gg)
                            av(gg - 1, v_b)
                        if kt % 4 == 3 and fi < len(fill):
                            fill[fi]()
                            fi += 1
                    te.wait_ge(s_exp, (pg + 1) * NKT)
                    av((pg + 1) * NKT - 1, v_b)
                while fi < len(fill):
                    fill[fi]()
                    fi += 1
            if STAGE != "B":
                for f in pe_items(te, QBN - 1):
                    f()
            if STAGE == "full":
                te.wait_ge(s_oev, NOB * QBN)
            elif STAGE == "BCD":
                te.wait_ge(s_rt, 12)
            elif STAGE in ("BC", "BCD1", "BCT", "BCSN"):
                te.wait_ge(s_cev, 32)
            else:
                te.wait_ge(s_nst, 32)
            te.sem_inc(s_ph, 1)

        @block.scalar
        def _(sc):
            sc.wait_ge(s_ph, PH)
            for qb in range(QBN):
                for gl in range(NGQ):
                    gg = qb * NGQ + gl
                    sc.wait_ge(s_sc, gg + 1)
                    if gg >= 4:
                        sc.wait_ge(s_av, gg - 3)
                    sc.activation(P_buf[:, (gg % 4) * 1024:(gg % 4 + 1) * 1024],
                                  ps_S[gg % 2][:, :], AF.Exp,
                                  bias=biasC[:, :], scale=0.125).then_inc(s_exp, 1)
                    if qb > 0 and STAGE in ("BCD", "BCD1", "full") and gl == 48:
                        sc.wait_ge(s_d2s, qb)
                        sc.activation(d2_lt[:, :], d2_v[:, :], AF.Ln,
                                      bias=epsc[:, :], scale=1.0)
                        sc.activation(d2_r[:, :], d2_lt[:, :], AF.Exp, bias=0.0,
                                      scale=-0.5).then_inc(s_dr, 1)
                    if qb > 0 and STAGE in ("BCD", "full") and gl == 88:
                        sc.wait_ge(s_rt, 3 * (qb - 1) + 1)
                        sc.activation(ecol[:, :], dcol[:, :], AF.Exp, bias=0.0,
                                      scale=1.0).then_inc(s_rt, 1)
            if STAGE in ("BCD", "BCD1", "full"):
                sc.wait_ge(s_d2s, QBN)
                sc.activation(d2_lt[:, :], d2_v[:, :], AF.Ln, bias=epsc[:, :], scale=1.0)
                sc.activation(d2_r[:, :], d2_lt[:, :], AF.Exp, bias=0.0,
                              scale=-0.5).then_inc(s_dr, 1)
            if STAGE in ("BCD", "full"):
                sc.wait_ge(s_rt, 3 * (QBN - 1) + 1)
                sc.activation(ecol[:, :], dcol[:, :], AF.Exp, bias=0.0,
                              scale=1.0).then_inc(s_rt, 1)
            sc.sem_inc(s_ph, 1)

        @block.vector
        def _(v):
            v.wait_ge(s_ph, PH)

            def nrm(pg):
                p = pg % PAIRS
                v.wait_ge(s_av, NKT * (pg + 1))
                v.tensor_copy(OTF32[0:VS, 0:512], ps_OA[0:VS, :])
                v.tensor_copy(OTF32[0:VS, 512:1024],
                              ps_OB[0:VS, :]).then_inc(s_nrm, 1)
                v.reciprocal(rinv[64:65, 0:512], OTF32[64:65, 0:512])
                v.reciprocal(rinv[64:65, 512:1024],
                             OTF32[64:65, 512:1024]).then_inc(s_riv, 1)
                v.wait_ge(s_rbc, 32 * (pg + 1))
                if pg >= 2:
                    v.wait_ge(s_ots, 32 * (pg - 1))
                v.tensor_tensor(OT_stage[pg % 2][:, 0:512],
                                OTF32[0:64, 0:512], rb_sb[:, 0:512], ALU.mult)
                v.tensor_tensor(OT_stage[pg % 2][:, 512:1024],
                                OTF32[0:64, 512:1024], rb_sb[:, 512:1024],
                                ALU.mult).then_inc(s_nst, 1)

            for qb in range(QBN):
                vf = v_items(v, qb - 1) if (qb > 0 and STAGE != "B") else []
                vi = 0
                for p in range(PAIRS):
                    pg = qb * PAIRS + p
                    lim = NKT * (p + 1)
                    while vi < len(vf) and vf[vi][0] < lim:
                        vf[vi][1]()
                        vi += 1
                    nrm(pg)
                while vi < len(vf):
                    vf[vi][1]()
                    vi += 1
            if STAGE != "B":
                for key, f in v_items(v, QBN - 1):
                    f()
            v.sem_inc(s_ph, 1)

        @block.sync
        def _(sync):
            sync.wait_ge(s_ph, PH)
            sync.dma_start(out=bobc[:, :],
                           in_=bass.AP(bo, 0, [[0, 128], [1, D]])).then_inc(s_bo, 16)
            for qb in range(QBN):
                if qb >= 2:
                    sync.wait_ge(s_sc, NGQ * (qb - 1))
                sync.dma_start(out=QT_b[qb % 2][:, :].rearrange(
                                   "p (pr c) -> p pr c", pr=PAIRS),
                               in_=QT_dram.ap()[:, :, qb * 512:(qb + 1) * 512],
                               ).then_inc(s_qld, 16)
                if qb > 0 and STAGE not in ("B", "BC", "BCSN"):
                    for ti in range(4):
                        sync.wait_ge(s_ln2, 4 * (qb - 1) + ti + 1)
                        sync.dma_start_transpose(
                            out=xn2[:, :, ti * 128:(ti + 1) * 128],
                            in_=xn2_t[ti % 2][:, :]).then_inc(s_xn2, 16)
                for p in range(PAIRS):
                    pg = qb * PAIRS + p
                    sync.wait_ge(s_riv, pg + 1)
                    sync.dma_start(out=rv_dram.ap()[pg % 2:pg % 2 + 1, :],
                                   in_=rinv[64:65, 0:1024]).then_inc(s_rbc, 16)
                    sync.wait_ge(s_rbc, 32 * pg + 16)
                    sync.dma_start(out=rb_sb[:, :],
                                   in_=bass.AP(rv_dram, (pg % 2) * D,
                                               [[0, 64], [1, 1024]]),
                                   ).then_inc(s_rbc, 16)
                    sync.wait_ge(s_nst, pg + 1)
                    sync.dma_start(out=OT_pack[qb % 2][0:64, p * 512:(p + 1) * 512],
                                   in_=OT_stage[pg % 2][:, 0:512]).then_inc(s_ots, 16)
                    sync.dma_start(out=OT_pack[qb % 2][64:128, p * 512:(p + 1) * 512],
                                   in_=OT_stage[pg % 2][:, 512:1024]).then_inc(s_ots, 16)
                if qb > 0:
                    for tt in range(4):
                        if STAGE == "full":
                            sync.wait_ge(s_oev, NOB * (qb - 1) + 66 + 2 * tt)
                        elif STAGE == "BCD":
                            sync.wait_ge(s_rt, 3 * qb)
                        elif STAGE in ("BC", "BCD1", "BCT", "BCSN"):
                            sync.wait_ge(s_cev, 8 * qb)
                        else:
                            sync.wait_ge(s_nst, 8 * qb)
                        sync.dma_start(out=out_r[:, (qb - 1) * 4 + tt, :],
                                       in_=x1r[:, tt, :]).then_inc(s_od, 16)
                sync.wait_ge(s_od, 64 * qb)
                sync.dma_start(
                    out=x1[:, :].rearrange("p (a d) -> p a d", a=4),
                    in_=xb.ap().rearrange("(a p) d -> p a d", p=128)[:, qb * 4:(qb + 1) * 4, :],
                ).then_inc(s_x1l, 16)
            if STAGE not in ("B", "BC", "BCSN"):
                for ti in range(4):
                    sync.wait_ge(s_ln2, 4 * (QBN - 1) + ti + 1)
                    sync.dma_start_transpose(
                        out=xn2[:, :, ti * 128:(ti + 1) * 128],
                        in_=xn2_t[ti % 2][:, :]).then_inc(s_xn2, 16)
            for tt in range(4):
                if STAGE == "full":
                    sync.wait_ge(s_oev, NOB * (QBN - 1) + 66 + 2 * tt)
                elif STAGE == "BCD":
                    sync.wait_ge(s_rt, 12)
                elif STAGE in ("BC", "BCD1", "BCT", "BCSN"):
                    sync.wait_ge(s_cev, 32)
                else:
                    sync.wait_ge(s_nst, 32)
                sync.dma_start(out=out_r[:, (QBN - 1) * 4 + tt, :],
                               in_=x1r[:, tt, :]).then_inc(s_od, 16)

        @block.gpsimd
        def _(gp):
            gp.wait_ge(s_ph, PH)
            gp.dma_start(
                out=WoT_sb[:, :].rearrange("p (pr d) -> p pr d", pr=PAIRS),
                in_=WoT.ap().rearrange("(pr p) d -> p pr d", p=128),
            ).then_inc(s_wc, 16)
            gp.dma_start(
                out=Wr_sb[:, :].rearrange("p (cc e) -> p cc e", cc=CC),
                in_=Wr.ap().rearrange("(cc p) e -> p cc e", p=128),
            ).then_inc(s_wc, 16)
            W1src = [W1.ap()[ei] for ei in range(8)] + [Ws1.ap()]
            W2src = [W2.ap()[ei] for ei in range(8)] + [Ws2.ap()]
            def wload(cq, e):
                we = cq * NE + e
                if cq == 0:
                    gp.wait_ge(s_stg, 32 * KB + 32 * (e + 1))
                if we >= 2:
                    gp.wait_ge(s_hmm, FC * (we - 1))
                gp.dma_start(out=W1_sb[we % 2][:, :],
                             in_=Wf1_dram.ap()[e]).then_inc(s_wld, 16)
                if we >= 1:
                    gp.wait_ge(s_omm, 8 * we)
                gp.dma_start(out=W2_sb[:, :],
                             in_=Wf2_dram.ap()[e]).then_inc(s_w2l, 16)

            for qb in range(QBN):
                for p in range(PAIRS):
                    pg = qb * PAIRS + p
                    if pg >= 2:
                        gp.wait_ge(s_av, NKT * (pg - 1))
                    gp.dma_start(out=KT_buf[pg % 2][:, :],
                                 in_=KT_dram.ap()[:, p, :]).then_inc(s_kvld, 16)
                    gp.dma_start(
                        out=V_buf[pg % 2][:, :].rearrange("p (h c) -> p h c", h=2),
                        in_=V_dram.ap()[:, 2 * p:2 * p + 2, :],
                    ).then_inc(s_kvld, 16)
                    if qb == 0:
                        for e in ([p] if p < 7 else [7, 8]):
                            gp.dma_start(
                                out=Wf1_dram.ap()[e].rearrange("p (cc f) -> p cc f", cc=CC),
                                in_=W1src[e].rearrange("(cc p) f -> p cc f", p=128),
                            ).then_inc(s_stg, 16)
                            gp.dma_start(
                                out=Wf2_dram.ap()[e].rearrange("p (fc d) -> p fc d", fc=FC),
                                in_=W2src[e].rearrange("(fc p) d -> p fc d", p=128),
                            ).then_inc(s_stg, 16)
                    elif STAGE == "full":
                        for e in ([p] if p < 7 else [7, 8]):
                            wload(qb - 1, e)
            if STAGE == "full":
                for e in range(NE):
                    wload(QBN - 1, e)
            gp.wait_ge(s_od, 64 * QBN)
            gp.sem_inc(s_ph, 1)

        L.close()
    return nc


_PROG_CACHE = {}


def _get_program():
    if "p" not in _PROG_CACHE:
        _PROG_CACHE["p"] = build_program()
    return _PROG_CACHE["p"]


def make_in_maps(x, WqT, WkT, WvT, WoT, bo, Ws1, Ws2, W1, W2, Wr):
    in_maps = []
    for core in range(8):
        bidx, qidx = divmod(core, 4)
        xb_c = np.ascontiguousarray(np.roll(x[bidx], -qidx * NQ, axis=0))
        in_maps.append({
            "xb": xb_c, "WqT": WqT, "WkT": WkT, "WvT": WvT, "WoT": WoT,
            "bo": bo, "Ws1": Ws1, "Ws2": Ws2, "W1": W1, "W2": W2, "Wr": Wr,
        })
    return in_maps


def prep_inputs(L0_attn_x, L0_attn_W_q, L0_attn_W_k, L0_attn_W_v, L0_attn_W_o,
                L0_attn_b_o, L0_ffn_Ws1, L0_ffn_Ws2, L0_ffn_W1, L0_ffn_W2,
                L0_ffn_Wr):
    f = lambda v: np.ascontiguousarray(np.asarray(v, dtype=np.float32))
    ft = lambda v: np.ascontiguousarray(np.asarray(v, dtype=np.float32).T)
    return (f(L0_attn_x), ft(L0_attn_W_q), ft(L0_attn_W_k), ft(L0_attn_W_v),
            ft(L0_attn_W_o), f(L0_attn_b_o), f(L0_ffn_Ws1), f(L0_ffn_Ws2),
            f(L0_ffn_W1), f(L0_ffn_W2), f(L0_ffn_Wr))


def kernel(**inputs):
    arrs = prep_inputs(**inputs)
    nc = _get_program()
    in_maps = make_in_maps(*arrs)
    res = run_bass_kernel_spmd(nc, in_maps, core_ids=list(range(8)))
    out = np.empty((2, LK, D), dtype=np.float32)
    for core in range(8):
        bidx, qidx = divmod(core, 4)
        out[bidx, qidx * NQ:(qidx + 1) * NQ] = np.asarray(
            res.results[core]["out"], dtype=np.float32)
    return out
